# version 58 (final)
# speedup vs baseline: 1.2147x; 1.0058x over previous
import contextlib
import numpy as np
import concourse.bass as bass
import concourse.mybir as mybir
from concourse.bass_utils import run_bass_kernel_spmd

F32 = mybir.dt.float32
BF16 = mybir.dt.bfloat16
AF = mybir.ActivationFunctionType
ALU = mybir.AluOpType
AX = mybir.AxisListType

D = 2048
S = 4096
OWN = 2048
LAT = 512
RD = 64
NH = 16
CH = 512
DT = D // 128
LT = LAT // 128
NM = OWN // 128
SCALE = float(1.0 / np.sqrt(128 + RD))
EPS = 1e-5
CHUNKSETS = [[0, 3, 4, 7], [1, 2, 5, 6]]
ER = 16
PVLAG = 2
STAGE = 99
SUB = 99

SEM_NAMES = [
    "sWL", "sHFa", "sHFb", "sVA", "sSA", "sST", "sLN1", "sLSa", "sLSb",
    "sT1a", "sT1b", "sBmm", "sBev", "sWK", "sWV", "sKRg", "sGq",
    "sGt", "sGs", "sGv", "sGg",
    "sDWa", "sDWb", "sDV", "sDM", "sDT", "sWQa", "sWQb", "sBH",
    "sEKa", "sEKb", "sSCt", "sEX", "sGM", "sVU", "sPVt", "sUB", "sSM", "sSL",
    "sRD", "sON",
    "sWOL", "sHOa", "sHOb", "sFM", "sX1", "sIN2", "sW2a", "sW2b",
    "sXWa", "sXWb",
    "sWQL", "sWQH", "sL2T", "sBF", "sGMM", "sXRa", "sXRb", "sOUT", "sFNa", "sFNb",
    "sMS", "sWKe", "sWVe", "sVC", "sWa", "sWb", "sWc", "sWd", "sWOC", "sWOH", "sLta", "sLtc", "sTP", "sTC", "sWQ2", "sWQ3", "sGV", "sQW", "sQb", "sQc", "sDWc", "sDWd",
]


class Counts:
    def __init__(self):
        self.v = {}

    def inc(self, name, k=1):
        self.v[name] = self.v.get(name, 0) + k
        return self.v[name]

    def get(self, name):
        return self.v.get(name, 0)


def _rows3(ap, p=128):
    return ap.rearrange("(kt p) l -> p kt l", p=p)


def _sb3(ap, kt):
    return ap.rearrange("p (kt l) -> p kt l", kt=kt)



def _mark(en, eng, sem, k=1, scr=None):
    if en == "tensor":
        eng.nop().then_inc(sem, k)
    elif en == "scalar":
        eng.memzero(scr).then_inc(sem, k)
    else:
        eng.engine_nop().then_inc(sem, k)


def build_nc(nblocks=5):
    nc = bass.Bass("TRN2", target_bir_lowering=False, debug=False)

    def din(name, shape, dtype=F32):
        return nc.dram_tensor(name, shape, dtype, kind="ExternalInput").ap()

    def dint(name, shape, dtype):
        return nc.dram_tensor(name, shape, dtype, kind="Internal").ap()

    hf = din("hf", [S, D])
    ho = din("ho", [OWN, D])
    wdkv = din("wdkv", [D, LAT])
    wuk = din("wuk", [LAT, D])
    wkr = din("wkr", [D, RD])
    wkrs = din("wkrs", [D, RD])
    wuv = din("wuv", [LAT, D])
    wdq = din("wdq", [D, LAT])
    wuq = din("wuq", [LAT, D])
    wqr = din("wqr", [LAT, NH * RD])
    wqrs = din("wqrs", [LAT, NH * RD])
    wo_d = din("wo", [D, D])
    eye_d = din("eye", [128, 128])
    wff = din("wff", [D, D])
    lnwc_d = din("lnwc", [128, DT])
    lnbc_d = din("lnbc", [128, DT])
    kcos_d = din("kcos", [RD, S])
    ksin_d = din("ksin", [RD, S])
    qcos_d = din("qcos", [128, OWN])
    qsin_d = din("qsin", [128, OWN])
    tri_d = din("tri", [128, 4 * CH])
    onesf_d = din("onesf", [128, 1])
    out_d = nc.dram_tensor("out", [OWN, D], F32, kind="ExternalOutput").ap()

    wo_bh = dint("wo_bh", [D, D], BF16)
    wdq_bh = dint("wdq_bh", [D, LAT], BF16)
    wuq_bh = dint("wuq_bh", [LAT, D], BF16)
    wqr_bh = dint("wqr_bh", [LAT, NH * RD], BF16)
    wqrs_bh = dint("wqrs_bh", [LAT, NH * RD], BF16)
    ln1_h = dint("ln1_h", [S, D], BF16)
    ln1o_h = dint("ln1o_h", [OWN, D], BF16)
    kT_h = dint("kT_h", [NH, 128, S], BF16)
    v_h = dint("v_h", [NH, 128, S], BF16)
    qT_h = dint("qT_h", [NH, 128, OWN], BF16)
    qrT_h = dint("qrT_h", [NH, RD, OWN], BF16)
    wq_h = dint("wq_h", [D, D], BF16)
    ln2_h = dint("ln2_h", [OWN, D], BF16)
    x1_h = dint("x1_h", [OWN, D], F32)
    bias_h = dint("bias_h", [128, D], F32)

    CTR = {en: Counts() for en in ("sync", "gpsimd", "tensor", "scalar", "vector")}

    top = contextlib.ExitStack()

    def sb(st, name, shape, dtype):
        return st.enter_context(nc.sbuf_tensor(name, shape, dtype))

    def ps(st, name, shape):
        return st.enter_context(nc.psum_tensor(name, shape, F32))

    with top:
        sems = {nm: top.enter_context(nc.semaphore(nm)) for nm in SEM_NAMES}

        onesb = sb(top, "onesb", [128, 128], BF16)
        onesf32 = sb(top, "onesf32", [128, 1], F32)
        onesrow = sb(top, "onesrow", [1, 128], F32)
        rsb = sb(top, "rsb", [128, 1], F32)
        lnwcb = sb(top, "lnwcb", [128, DT], F32)
        lnbcb = sb(top, "lnbcb", [128, DT], F32)
        lnblw = sb(top, "lnblw", [128, DT], F32)
        dmyV = sb(top, "dmyV", [128, D], BF16)
        dmyS = sb(top, "dmyS", [128, D], BF16)
        dmyS2 = sb(top, "dmyS2", [128, D], BF16)
        hbuf = [sb(top, f"hbuf{i}", [128, D], F32) for i in range(2)]
        lnst = [sb(top, f"lnst{i}", [128, D], BF16) for i in range(2)]
        statx = [sb(top, f"statx{i}", [128, 1], F32) for i in range(2)]
        statq = [sb(top, f"statq{i}", [128, 1], F32) for i in range(2)]
        stsc = [sb(top, f"stsc{i}", [128, 8], F32) for i in range(2)]
        kRb = sb(top, "kRb", [128, S], BF16)
        gscr = sb(top, "gscr", [1, 4], mybir.dt.int32)
        ideye = sb(top, "ideye", [128, 128], BF16)

        holder = {}
        holder["stg"] = [
            top.enter_context(nc.sbuf_tensor(f"stg{i}", [128, D], BF16,
                                             side="right"))
            for i in range(2)]

        emitters = []

        def emit_all():
            def fused(en, eng, C):
                for f in emitters:
                    f(en, eng, C)
            emit_block(fused)

        def emit_block(fn):
            with nc.Block() as block:
                @block.sync
                def _(e):
                    fn("sync", e, CTR["sync"])

                @block.gpsimd
                def _(e):
                    fn("gpsimd", e, CTR["gpsimd"])

                @block.tensor
                def _(e):
                    fn("tensor", e, CTR["tensor"])

                @block.scalar
                def _(e):
                    fn("scalar", e, CTR["scalar"])

                @block.vector
                def _(e):
                    fn("vector", e, CTR["vector"])

        def stats_chain(en, eng, C, b):
            if en == "vector":
                eng.wait_ge(sems["sSA"], C.get("SA"))
                eng.wait_ge(sems["sVA"], C.get("VA"))
                eng.tensor_scalar_mul(stsc[b][:, 0:1], statx[b][:, :], 1.0 / D
                                      ).then_inc(sems["sST"], 1)
                eng.wait_ge(sems["sST"], C.get("ST") + 1)
                eng.tensor_scalar_mul(stsc[b][:, 7:8], stsc[b][:, 0:1], -1.0)
                eng.tensor_scalar_mul(stsc[b][:, 1:2], statq[b][:, :], 1.0 / D)
                eng.tensor_tensor(stsc[b][:, 2:3], stsc[b][:, 0:1], stsc[b][:, 0:1],
                                  ALU.mult).then_inc(sems["sST"], 1)
                eng.wait_ge(sems["sST"], C.get("ST") + 2)
                eng.scalar_tensor_tensor(stsc[b][:, 3:4], stsc[b][:, 1:2], EPS,
                                         stsc[b][:, 2:3], ALU.add, ALU.subtract
                                         ).then_inc(sems["sST"], 1)
            C.inc("ST", 3)
            if en == "scalar":
                eng.wait_ge(sems["sST"], C.get("ST"))
                eng.activation(stsc[b][:, 4:5], stsc[b][:, 3:4], AF.Ln
                               ).then_inc(sems["sSA"], 1)
                eng.wait_ge(sems["sSA"], C.get("SA") + 1)
                eng.activation(stsc[b][:, 5:6], stsc[b][:, 4:5], AF.Exp, scale=-0.5
                               ).then_inc(sems["sSA"], 1)
                eng.wait_ge(sems["sSA"], C.get("SA") + 2)
            C.inc("SA", 2)

        def ln_apply(en, eng, C, b, src_ap, out_ap, out_sem):
            if en == "scalar":
                eng.activation(dmyS[:, :], src_ap, AF.Identity,
                               bias=stsc[b][:, 7:8]).then_inc(sems["sSA"], 1)
                eng.wait_ge(sems["sSA"], C.get("SA") + 1)
                eng.activation(out_ap, dmyS[:, :], AF.Identity,
                               scale=stsc[b][:, 5:6]).then_inc(sems[out_sem], 1)
            C.inc("SA", 1)
            C.inc(out_sem[1:], 1)

        def emit_ln_tile(en, eng, C, src_d, dst_d, lt):
            b = C.get("LNLD") % 2
            g = C.get("LNLD")
            hfs = "sHFa" if b == 0 else "sHFb"
            lss = "sLSa" if b == 0 else "sLSb"
            if en == "gpsimd":
                if g >= 2:
                    eng.wait_ge(sems["sLN1"], g - 1)
                eng.dma_start(hbuf[b][:, :], src_d[lt * 128:(lt + 1) * 128, :]
                              ).then_inc(sems[hfs], 16)
            C.inc(hfs[1:], 16)
            C.inc("LNLD", 1)
            if en == "vector":
                eng.wait_ge(sems[hfs], C.get(hfs[1:]))
                eng.tensor_scalar(dmyV[:, :], hbuf[b][:, :], 1.0, 0.0,
                                  ALU.mult, ALU.add,
                                  accum_out=statx[b][:, :]).then_inc(sems["sVA"], 1)
            C.inc("VA", 1)
            if en == "scalar":
                eng.wait_ge(sems[hfs], C.get(hfs[1:]))
                eng.activation(dmyS2[:, :], hbuf[b][:, :], AF.Square,
                               accum_out=statq[b][:, :]).then_inc(sems["sSA"], 1)
            C.inc("SA", 1)
            stats_chain(en, eng, C, b)
            if en == "scalar":
                eng.wait_ge(sems[lss], 16 * C.get(lss[1:]))
            ln_apply(en, eng, C, b, hbuf[b][:, :], lnst[b][:, :], "sLN1")
            if en == "scalar":
                eng.wait_ge(sems["sLN1"], C.get("LN1"))
                eng.dma_start(dst_d[lt * 128:(lt + 1) * 128, :], lnst[b][:, :]
                              ).then_inc(sems[lss], 16)
            C.inc(lss[1:], 1)

        def make_b1():
            s1 = holder["s1"]
            wAb, wBb, wCb, wDb, wEb = (holder[k] for k in ("wA", "wB", "wC", "wD", "wE"))
            kcosb, ksinb, qcosb, qsinb = (holder[k] for k in
                                          ("kcosb", "ksinb", "qcosb", "qsinb"))
            lnT, ckvT, est, vst = (holder[k] for k in ("lnT", "ckvT", "est", "vst"))
            krst, krt1, krt2 = (holder[k] for k in ("krst", "krt1", "krt2"))
            psP, psQ = holder["psP"], holder["psQ"]

            WLOADS1 = [
                (_sb3(wAb[...], DT), _rows3(wdkv), "sWa"),
                (_sb3(wBb[...], LT), _rows3(wuk), "sWb"),
                (_sb3(wCb[...], LT), _rows3(wuv), "sWc"),
                (_sb3(wDb[...], DT), _rows3(wkr), "sWd"),
                (_sb3(wEb[...], DT), _rows3(wkrs), "sWd"),
                (kcosb[...], kcos_d, "sWL"), (ksinb[...], ksin_d, "sWL"),
                (qcosb[...], qcos_d, "sWL"), (qsinb[...], qsin_d, "sWL"),
            ]
            NR = NH * RD
            WLOADS2 = [
                (_sb3(wAb[...], DT), _rows3(wdq), "sQW"),
                (_sb3(wBb[...], LT), _rows3(wuq), "sQb"),
                (_sb3(wCb[:, 0:LT * NR], LT), _rows3(wqr), "sQc"),
                (_sb3(wCb[:, LT * NR:2 * LT * NR], LT), _rows3(wqrs),
                 "sQc"),
            ]

            def second_level_16(en, eng, C, cb, dst_h, col0, pump=None):
                for rnd in range(4):
                    pbuf = psQ if rnd % 2 == 0 else psP
                    fB = "fB0" if rnd % 2 == 0 else "fB1"
                    if pump and rnd in (1, 3):
                        pump()
                    if en == "tensor":
                        if C.get("WbW") == 0:
                            eng.wait_ge(sems["sWb"], 16)
                        if dst_h is qT_h and C.get("QbW") == 0:
                            eng.wait_ge(sems["sQb"], 16)
                        eng.wait_ge(sems["sBev"], C.get("evP"))
                        if C.get(fB):
                            eng.wait_ge(sems["sBev"], C.get(fB))
                        for mm_ in range(4):
                            m = rnd * 4 + mm_
                            for k in range(LT):
                                mi = eng.matmul(pbuf[:, mm_ * CH:(mm_ + 1) * CH],
                                           wBb[:, k * D + m * 128:k * D + (m + 1) * 128],
                                           ckvT[cb][:, k * CH:(k + 1) * CH],
                                           start=(k == 0), stop=(k == LT - 1))
                        mi.then_inc(sems["sBmm"], 1)
                    C.inc("Bmm", 1)
                    C.inc("WbW", 1)
                    if dst_h is qT_h:
                        C.inc("QbW", 1)
                    s0 = C.get("ESTi") % 16
                    for mm_ in range(4):
                        ei = C.get("ESTi") % 16
                        C.inc("ESTi", 1)
                        if en == "vector":
                            eng.wait_ge(sems["sBmm"], C.get("Bmm"))
                            if rnd == 0 and mm_ == 0 and C.get("WKe"):
                                eng.wait_ge(sems["sWKe"], C.get("WKe"))
                            eng.tensor_copy(est[:, ei * CH:(ei + 1) * CH],
                                            pbuf[:, mm_ * CH:(mm_ + 1) * CH]
                                            ).then_inc(sems["sBev"], 1)
                        C.inc("Bev", 1)
                    C.v[fB] = C.get("Bev")
                    m0 = rnd * 4
                    if en == "scalar":
                        eng.wait_ge(sems["sBev"], C.get("Bev"))
                        eng.dma_start(
                            dst_h[m0:m0 + 4, :, col0:col0 + CH]
                            .rearrange("g p j -> p g j"),
                            est[:, s0 * CH:(s0 + 4) * CH]
                            .rearrange("p (g j) -> p g j", g=4)
                            ).then_inc(sems["sWK"], 16)
                    C.inc("WK", 16)
                    C.v["evQ"] = C.get("Bev")
                if en == "scalar":
                    eng.wait_ge(sems["sWK"], C.get("WK"))
                    eng.memzero(gscr[0:1, 0:1]).then_inc(sems["sWKe"], 1)
                C.inc("WKe", 1)

            def proj_kv_tail(en, eng, C, c, cb, pump=None):
                second_level_16(en, eng, C, cb, kT_h, c * CH, pump)
                for m in range(4):
                    pbuf = psQ if m % 2 == 0 else psP
                    fB = "fB0" if m % 2 == 0 else "fB1"
                    if pump and m in (1, 3):
                        pump()
                    if en == "tensor":
                        if C.get("WcW") == 0:
                            eng.wait_ge(sems["sWc"], 16)
                        if C.get(fB):
                            eng.wait_ge(sems["sBev"], C.get(fB))
                        for n in range(4):
                            for k in range(LT):
                                mi = eng.matmul(pbuf[:, n * CH:(n + 1) * CH],
                                           ckvT[cb][:, k * CH + m * 128:
                                                     k * CH + (m + 1) * 128],
                                           wCb[:, k * D + n * CH:k * D + (n + 1) * CH],
                                           start=(k == 0), stop=(k == LT - 1))
                        mi.then_inc(sems["sBmm"], 1)
                    C.inc("Bmm", 1)
                    C.inc("WcW", 1)
                    if en == "vector":
                        eng.wait_ge(sems["sBmm"], C.get("Bmm"))
                        if C.get("Vdn") >= 2:
                            eng.wait_ge(sems["sWVe"], C.get("Vdn") - 1)
                        eng.tensor_copy(vst[m % 2][:, :], pbuf[:, :]
                                        ).then_inc(sems["sBev"], 1)
                    C.inc("Bev", 1)
                    C.inc("Vdn", 1)
                    C.v[fB] = C.get("Bev")
                    tg = c * 4 + m
                    for hg in range(2):
                        if en == "scalar":
                            eng.wait_ge(sems["sBev"], C.get("Bev"))
                            eng.dma_start(
                                v_h[hg * 8:(hg + 1) * 8, :, tg * 128:(tg + 1) * 128]
                                .rearrange("g p j -> p g j"),
                                vst[m % 2][:, hg * 1024:(hg + 1) * 1024]
                                .rearrange("p (g j) -> p g j", g=8)
                                ).then_inc(sems["sWV"], 16)
                        C.inc("WV", 16)
                    if en == "scalar":
                        eng.wait_ge(sems["sWV"], C.get("WV"))
                        eng.memzero(gscr[0:1, 0:1]).then_inc(sems["sWVe"], 1)
                    C.inc("WVe", 1)
                if en == "tensor":
                    if C.get("WdW") == 0:
                        eng.wait_ge(sems["sWd"], 32)
                    if C.get("fB0"):
                        eng.wait_ge(sems["sBev"], C.get("fB0"))
                    if C.get("KRg"):
                        eng.wait_ge(sems["sKRg"], C.get("KRg"))
                    for k in range(DT):
                        eng.matmul(psQ[0:RD, 2 * CH:3 * CH],
                                   wDb[:, k * RD:(k + 1) * RD],
                                   lnT[cb][:, k * CH:(k + 1) * CH],
                                   start=(k == 0), stop=(k == DT - 1))
                    for k in range(DT):
                        mi = eng.matmul(psQ[0:RD, 3 * CH:4 * CH],
                                   wEb[:, k * RD:(k + 1) * RD],
                                   lnT[cb][:, k * CH:(k + 1) * CH],
                                   start=(k == 0), stop=(k == DT - 1))
                    mi.then_inc(sems["sBmm"], 1)
                C.inc("Bmm", 1)
                C.inc("WdW", 1)
                if en == "vector":
                    eng.wait_ge(sems["sBmm"], C.get("Bmm"))
                    eng.tensor_copy(krst[:, :], psQ[0:RD, 2 * CH:4 * CH]
                                    ).then_inc(sems["sBev"], 1)
                    eng.wait_ge(sems["sBev"], C.get("Bev") + 1)
                    if C.get("TblW") == 0:
                        eng.wait_ge(sems["sWL"], C.get("WLinit"))
                    eng.tensor_tensor(krt1[0:RD, :], krst[:, 0:CH],
                                      kcosb[:, c * CH:(c + 1) * CH], ALU.mult)
                    eng.tensor_tensor(krt2[0:RD, :], krst[:, CH:2 * CH],
                                      ksinb[:, c * CH:(c + 1) * CH], ALU.mult
                                      ).then_inc(sems["sVC"], 1)
                    eng.wait_ge(sems["sVC"], C.inc("VC"))
                    eng.tensor_tensor(kRb[0:RD, c * CH:(c + 1) * CH], krt1[0:RD, :],
                                      krt2[0:RD, :], ALU.add).then_inc(sems["sKRg"], 1)
                C.inc("TblW", 1)
                C.inc("Bev", 1)
                C.inc("KRg", 1)
                C.v["fB0"] = C.get("Bev")

            def proj_q_tail(en, eng, C, c, cb, pump=None):
                second_level_16(en, eng, C, cb, qT_h, c * CH, pump)
                for hp in range(NH // 2):
                    pbuf = psQ if hp % 2 == 0 else psP
                    fB = "fB0" if hp % 2 == 0 else "fB1"
                    if pump and hp in (2, 5):
                        pump()
                    if en == "tensor":
                        if C.get("QcW") == 0:
                            eng.wait_ge(sems["sQc"], 32)
                        if C.get(fB):
                            eng.wait_ge(sems["sBev"], C.get(fB))
                        for k in range(LT):
                            eng.matmul(pbuf[:, 0:CH],
                                       wCb[:, k * NR + hp * 128:k * NR + (hp + 1) * 128],
                                       ckvT[cb][:, k * CH:(k + 1) * CH],
                                       start=(k == 0), stop=(k == LT - 1))
                        for k in range(LT):
                            mi = eng.matmul(pbuf[:, CH:2 * CH],
                                       wCb[:, LT * NR + k * NR + hp * 128:
                                           LT * NR + k * NR + (hp + 1) * 128],
                                       ckvT[cb][:, k * CH:(k + 1) * CH],
                                       start=(k == 0), stop=(k == LT - 1))
                        mi.then_inc(sems["sBmm"], 1)
                    C.inc("Bmm", 1)
                    C.inc("QcW", 1)
                    ei = C.get("ESTi") % 16
                    ei2 = (C.get("ESTi") + 1) % 16
                    C.inc("ESTi", 2)
                    if en == "vector":
                        eng.wait_ge(sems["sBmm"], C.get("Bmm"))
                        if hp == 0 and C.get("WKe"):
                            eng.wait_ge(sems["sWKe"], C.get("WKe"))
                        eng.tensor_copy(est[:, ei * CH:(ei + 1) * CH], pbuf[:, 0:CH])
                        eng.tensor_copy(est[:, ei2 * CH:(ei2 + 1) * CH], pbuf[:, CH:2 * CH]
                                        ).then_inc(sems["sBev"], 1)
                        eng.wait_ge(sems["sBev"], C.get("Bev") + 1)
                        eng.tensor_tensor(krt1[:, :], est[:, ei * CH:(ei + 1) * CH],
                                          qcosb[:, c * CH:(c + 1) * CH], ALU.mult)
                        eng.tensor_tensor(krt2[:, :], est[:, ei2 * CH:(ei2 + 1) * CH],
                                          qsinb[:, c * CH:(c + 1) * CH], ALU.mult
                                          ).then_inc(sems["sVC"], 1)
                        eng.wait_ge(sems["sVC"], C.inc("VC"))
                        eng.tensor_tensor(est[:, ei * CH:(ei + 1) * CH], krt1[:, :],
                                          krt2[:, :], ALU.add).then_inc(sems["sBev"], 1)
                    C.inc("Bev", 2)
                    C.v[fB] = C.get("Bev")
                    for sub in range(2):
                        h = hp * 2 + sub
                        if en == "scalar":
                            eng.wait_ge(sems["sBev"], C.get("Bev"))
                            eng.dma_start(qrT_h[h, :, c * CH:(c + 1) * CH],
                                          est[sub * RD:(sub + 1) * RD,
                                              ei * CH:(ei + 1) * CH]
                                          ).then_inc(sems["sWK"], 16)
                        C.inc("WK", 16)
                if en == "scalar":
                    eng.wait_ge(sems["sWK"], C.get("WK"))
                    eng.memzero(gscr[0:1, 0:1]).then_inc(sems["sWKe"], 1)
                C.inc("WKe", 1)

            def proj_chunk_loads(en, eng, C, c, kv):
                cb = c % 2
                tag = c if kv else 1000 + c
                prev_tag = (tag - 2) if c >= 2 else (6 + c if not kv else None)
                t1s = "sT1a" if cb == 0 else "sT1b"
                if en == "sync":
                    if prev_tag is not None:
                        eng.wait_ge(sems["sBmm"], C.get("BmmChunk%d" % prev_tag))
                    if kv:
                        na = (4 * (c + 1) + 1) // 2
                        nb = (4 * (c + 1)) // 2
                        eng.wait_ge(sems["sLSa"], 16 * na)
                        eng.wait_ge(sems["sLSb"], 16 * nb)
                        for k in range(DT):
                            eng.dma_start(lnT[cb][:, k * CH:(k + 1) * CH],
                                          ln1_h[c * CH:(c + 1) * CH,
                                                k * 128:(k + 1) * 128],
                                          transpose=True).then_inc(sems[t1s], 16)
                    else:
                        eng.wait_ge(sems["sLSa"], 16 * 16)
                        eng.wait_ge(sems["sLSb"], 16 * 16)
                        pid = eng.partition_id()
                        idx = eng.compute_val(pid % 2)
                        for arm in eng.Switch(idx, 2):
                            gc = CHUNKSETS[arm][c]
                            for k in range(DT):
                                eng.dma_start(lnT[cb][:, k * CH:(k + 1) * CH],
                                              ln1_h[gc * CH:(gc + 1) * CH,
                                                    k * 128:(k + 1) * 128],
                                              transpose=True
                                              ).then_inc(sems[t1s], 16)
                C.inc(t1s[1:], 16 * DT)
                C.v["T1chunk%d" % tag] = C.get(t1s[1:])

            def proj_chunk(en, eng, C, c, kv, pump=None):
                cb = c % 2
                tag = c if kv else 1000 + c
                t1s = "sT1a" if cb == 0 else "sT1b"
                if en == "tensor":
                    eng.wait_ge(sems[t1s], C.get("T1chunk%d" % tag))
                    if C.get("WaW") == 0:
                        eng.wait_ge(sems["sWa"], 16)
                    if not kv and C.get("QaW") == 0:
                        eng.wait_ge(sems["sQW"], 16)
                    if C.get("evP"):
                        eng.wait_ge(sems["sBev"], C.get("evP"))
                    if C.get("fB1"):
                        eng.wait_ge(sems["sBev"], C.get("fB1"))
                    for m in range(LT):
                        for k in range(DT):
                            mi = eng.matmul(psP[:, m * CH:(m + 1) * CH],
                                       wAb[:, k * LAT + m * 128:k * LAT + (m + 1) * 128],
                                       lnT[cb][:, k * CH:(k + 1) * CH],
                                       start=(k == 0), stop=(k == DT - 1))
                    mi.then_inc(sems["sBmm"], 1)
                C.inc("Bmm", 1)
                C.inc("WaW", 1)
                if not kv:
                    C.inc("QaW", 1)
                for m in range(LT):
                    if en == "vector":
                        eng.wait_ge(sems["sBmm"], C.get("Bmm"))
                        if C.get("Ck2%d" % cb):
                            eng.wait_ge(sems["sBmm"], C.get("Ck2%d" % cb))
                        eng.tensor_copy(ckvT[cb][:, m * CH:(m + 1) * CH],
                                        psP[:, m * CH:(m + 1) * CH]
                                        ).then_inc(sems["sBev"], 1)
                    C.inc("Bev", 1)
                C.v["evP"] = C.get("Bev")
                if pump:
                    pump()
                if kv:
                    proj_kv_tail(en, eng, C, c, cb, pump)
                else:
                    proj_q_tail(en, eng, C, c, cb, pump)
                C.v["BmmChunk%d" % tag] = C.get("Bmm")
                C.v["Ck2%d" % cb] = C.get("Bmm")

            def emit_b1(en, eng, C):
                for dst, src, snm in WLOADS1:
                    if en == "gpsimd":
                        eng.dma_start(dst, src).then_inc(sems[snm], 16)
                    C.inc(snm[1:], 16)
                if en == "gpsimd":
                    eng.dma_start(ideye[:, :], eye_d[:, :]).then_inc(sems["sWL"], 16)
                    eng.dma_start(onesf32[:, :], onesf_d).then_inc(sems["sWL"], 16)
                    eng.dma_start(lnwcb[:, :], lnwc_d).then_inc(sems["sWL"], 16)
                    eng.dma_start(lnbcb[:, :], lnbc_d).then_inc(sems["sWL"], 16)
                    eng.memset(onesb[:, :], 1.0)
                    eng.memset(onesrow[:, :], 1.0)
                    eng.memset(kRb[RD:128, :], 0.0).then_inc(sems["sMS"], 1)
                C.inc("WL", 64)
                C.inc("MS", 1)
                C.v["WLinit"] = C.get("WL")
                if en == "scalar":
                    eng.memzero(gscr[:, :]).then_inc(sems["sSA"], 1)
                    eng.wait_ge(sems["sSA"], C.get("SA") + 1)
                C.inc("SA", 1)
                if en == "tensor":
                    eng.wait_ge(sems["sMS"], C.get("MS"))
                NU = S // CH + OWN // CH

                def unit_ln(i):
                    if i < S // CH:
                        for j in range(4):
                            emit_ln_tile(en, eng, C, hf, ln1_h, 4 * i + j)
                    else:
                        for j in range(4):
                            emit_ln_tile(en, eng, C, ho, ln1o_h,
                                         4 * (i - S // CH) + j)

                def unit_loads(i):
                    if i < S // CH:
                        proj_chunk_loads(en, eng, C, i, True)
                    else:
                        proj_chunk_loads(en, eng, C, i - S // CH, False)

                def unit_compute(i):
                    if i == S // CH:
                        if en == "tensor":
                            eng.wait_ge(sems["sBmm"], C.get("Bmm"))
                            _mark(en, eng, sems["sGq"])
                        C.inc("Gq", 1)
                        for dst2, src2, s2m in WLOADS2:
                            if en == "gpsimd":
                                if C.get("WL2") == 0:
                                    eng.wait_ge(sems["sGq"], C.get("Gq"))
                                eng.dma_start(dst2, src2).then_inc(sems[s2m], 16)
                            C.inc(s2m[1:], 16)
                            C.inc("WL2", 16)
                    if i < S // CH:
                        proj_chunk(en, eng, C, i, True, pump)
                    else:
                        proj_chunk(en, eng, C, i - S // CH, False, pump)

                pending = []
                for i in range(S // CH):
                    for j in range(4):
                        pending.append((hf, ln1_h, 4 * i + j))

                def pump():
                    if pending:
                        a, d, lt = pending.pop(0)
                        emit_ln_tile(en, eng, C, a, d, lt)

                for _ in range(16):
                    pump()
                unit_loads(0)
                for i in range(NU):
                    if i + 1 < NU:
                        unit_loads(i + 1)
                    unit_compute(i)
                while pending:
                    pump()

                for nm, e2 in (("sGt", "tensor"), ("sGv", "vector"),
                               ("sGs", "scalar"), ("sGg", "gpsimd")):
                    if en == e2:
                        if en == "tensor":
                            eng.wait_ge(sems["sBmm"], C.get("Bmm"))
                        _mark(en, eng, sems[nm], scr=gscr[0:1, 2:3])
                    C.inc(nm[1:], 1)

            return emit_b1

        def make_b2():
            wtile, wzb, wm1, wm2 = (holder[k] for k in ("wtile", "wzb", "wm1", "wm2"))
            wqst, asum, atmp, brow = (holder[k] for k in
                                      ("wqst", "asum", "atmp", "brow"))
            psP, psQ = holder["psP"], holder["psQ"]

            def emit_b2(en, eng, C):
                for t in range(DT):
                    dws = ("sDWa", "sDWb", "sDWc", "sDWd")[t % 4]
                    ldq = "sync" if t % 2 == 0 else "gpsimd"
                    if en == ldq:
                        if t < 4:
                            for g in ("sGt", "sGv", "sGs"):
                                eng.wait_ge(sems[g], C.get(g[1:]))
                        if t >= 4:
                            if t % 2 == 0:
                                eng.wait_ge(sems["sDV"], C.get("DV") - 2)
                            else:
                                eng.wait_ge(sems["sSA"], C.get("SA") - 1)
                        eng.dma_start(wtile[t % 4][:, :],
                                      wff[t * 128:(t + 1) * 128, :]
                                      ).then_inc(sems[dws], 16)
                    C.inc(dws[1:], 16)
                    if t % 2 == 0:
                        if en == "vector":
                            eng.wait_ge(sems[dws], C.get(dws[1:]))
                            eng.wait_ge(sems["sDV"], C.get("DV"))
                            eng.reduce_sum(atmp[:, 0:1], wtile[t % 4][:, :],
                                           axis=AX.X, apply_absolute_value=True
                                           ).then_inc(sems["sDV"], 1)
                            eng.wait_ge(sems["sDV"], C.get("DV") + 1)
                            if t == 0:
                                eng.tensor_copy(asum[:, :], atmp[:, 0:1]
                                                ).then_inc(sems["sDV"], 1)
                            else:
                                eng.tensor_tensor(asum[:, :], asum[:, :],
                                                  atmp[:, 0:1], ALU.add
                                                  ).then_inc(sems["sDV"], 1)
                        C.inc("DV", 2)
                    else:
                        if en == "scalar":
                            eng.wait_ge(sems[dws], C.get(dws[1:]))
                            eng.wait_ge(sems["sSA"], C.get("SA"))
                            if t == 1:
                                eng.wait_ge(sems["sGv"], C.get("Gv"))
                            eng.activation(dmyS2[:, :], wtile[t % 4][:, :],
                                           AF.Abs,
                                           accum_out=stsc[0][:, t // 2:t // 2 + 1]
                                           ).then_inc(sems["sSA"], 1)
                        C.inc("SA", 1)
                if en == "vector":
                    eng.wait_ge(sems["sSA"], C.get("SA"))
                    eng.tensor_reduce(atmp[:, 1:2], stsc[0][:, 0:8], axis=AX.X,
                                      op=ALU.add).then_inc(sems["sDV"], 1)
                    eng.wait_ge(sems["sDV"], C.get("DV") + 1)
                    eng.tensor_tensor(asum[:, :], asum[:, :], atmp[:, 1:2],
                                      ALU.add).then_inc(sems["sDV"], 1)
                C.inc("DV", 2)
                if en == "tensor":
                    eng.wait_ge(sems["sWL"], C.get("WLinit"))
                    eng.wait_ge(sems["sDV"], C.get("DV"))
                    eng.matmul(psP[0:1, 0:1], asum[:, :], onesf32[:, :],
                               start=True, stop=True).then_inc(sems["sDT"], 1)
                C.inc("DTm", 1)
                if en == "vector":
                    eng.wait_ge(sems["sDT"], C.get("DTm"))
                    eng.reciprocal(atmp[0:1, 1:2], psP[0:1, 0:1]
                                   ).then_inc(sems["sDV"], 1)
                    eng.wait_ge(sems["sDV"], C.get("DV") + 1)
                    eng.tensor_scalar_mul(atmp[0:1, 2:3], atmp[0:1, 1:2],
                                          float(D) * float(D)).then_inc(sems["sDV"], 1)
                C.inc("DV", 2)
                if en == "tensor":
                    eng.wait_ge(sems["sDV"], C.get("DV"))
                    eng.matmul(psP[:, 4:5], onesrow[:, :], atmp[0:1, 2:3],
                               start=True, stop=True).then_inc(sems["sDT"], 1)
                C.inc("DTm", 1)
                if en == "vector":
                    eng.wait_ge(sems["sWL"], C.get("WLinit"))
                    eng.wait_ge(sems["sDT"], C.get("DTm"))
                    eng.tensor_copy(rsb[:, :], psP[:, 4:5]).then_inc(sems["sDV"], 1)
                    eng.tensor_copy(lnblw[:, :], lnbcb[:, :]).then_inc(sems["sDV"], 1)
                C.inc("DV", 2)
                for t in range(DT):
                    dws = ("sDWa", "sDWb", "sDWc", "sDWd")[t % 4]
                    ldq = "sync" if t % 2 == 0 else "gpsimd"
                    if en == ldq:
                        eng.wait_ge(sems["sDV"], C.get("DV"))
                        if t < 4:
                            eng.wait_ge(sems["sSA"], C.get("SA"))
                        if t >= 4:
                            eng.wait_ge(sems["sDM"], 2 * (t - 3))
                        eng.dma_start(wtile[t % 4][:, :],
                                      wff[t * 128:(t + 1) * 128, :]
                                      ).then_inc(sems[dws], 16)
                    C.inc(dws[1:], 16)
                    if en == "scalar":
                        eng.wait_ge(sems[dws], C.get(dws[1:]))
                        if t >= 1:
                            eng.wait_ge(sems["sDM"], 2 * t - 1)
                        eng.mul(wzb[:, :], wtile[t % 4][:, :], rsb[:, :]
                                ).then_inc(sems["sSA"], 1)
                    C.inc("SA", 1)
                    if en == "vector":
                        eng.wait_ge(sems["sSA"], C.get("SA"))
                        if t >= 1:
                            eng.wait_ge(sems["sDT"], C.get("DTm"))
                        if t >= 2:
                            wqs2 = "sWQa" if t % 2 == 0 else "sWQb"
                            eng.wait_ge(sems[wqs2], C.get(wqs2[1:]))
                        eng.tensor_single_scalar(wm1[:, :], wzb[:, :], 0.5, ALU.is_gt)
                        eng.tensor_single_scalar(wm2[:, :], wzb[:, :], -0.5, ALU.is_lt
                                                 ).then_inc(sems["sVC"], 1)
                        eng.wait_ge(sems["sVC"], C.inc("VC"))
                        eng.tensor_tensor(wm1[:, :], wm1[:, :], wm2[:, :],
                                          ALU.subtract).then_inc(sems["sDM"], 1)
                        eng.wait_ge(sems["sDM"], C.get("DM") + 1)
                        eng.tensor_scalar_mul(wqst[t % 2][:, :], wm1[:, :],
                                              lnwcb[:, t:t + 1]).then_inc(sems["sDM"], 1)
                    C.inc("DM", 2)
                    if en == "tensor":
                        eng.wait_ge(sems["sDM"], C.get("DM") - 1)
                        for n in range(4):
                            mi = eng.matmul(psQ[0:1, n * CH:(n + 1) * CH],
                                       lnblw[:, t:t + 1], wm1[:, n * CH:(n + 1) * CH],
                                       start=(t == 0), stop=(t == DT - 1))
                        mi.then_inc(sems["sDT"], 1)
                    C.inc("DTm", 1)
                    wqs = "sWQa" if t % 2 == 0 else "sWQb"
                    if en == "scalar":
                        eng.wait_ge(sems["sDM"], C.get("DM"))
                        eng.dma_start(wq_h[t * 128:(t + 1) * 128, :],
                                      wqst[t % 2][:, :]).then_inc(sems[wqs], 16)
                    C.inc(wqs[1:], 16)
                if en == "vector":
                    eng.wait_ge(sems["sDT"], C.get("DTm"))
                    eng.tensor_copy(brow[:, :], psQ[0:1, 0:D]).then_inc(sems["sDV"], 1)
                C.inc("DV", 1)
                if en == "tensor":
                    eng.wait_ge(sems["sDV"], C.get("DV"))
                    for n in range(4):
                        mi = eng.matmul(psP[:, n * CH:(n + 1) * CH], onesrow[:, :],
                                   brow[0:1, n * CH:(n + 1) * CH],
                                   start=True, stop=True)
                    mi.then_inc(sems["sDT"], 1)
                C.inc("DTm", 1)
                if en == "vector":
                    eng.wait_ge(sems["sDT"], C.get("DTm"))
                    eng.tensor_copy(wzb[:, :], psP[:, 0:D]).then_inc(sems["sDV"], 1)
                C.inc("DV", 1)
                if en == "gpsimd":
                    eng.wait_ge(sems["sDV"], C.get("DV"))
                    eng.dma_start(bias_h[:, :], wzb[:, :]).then_inc(sems["sBH"], 16)
                C.inc("BH", 16)
                for nm, e2 in (("sGt", "tensor"), ("sGv", "vector"),
                               ("sGs", "scalar"), ("sGg", "gpsimd")):
                    if en == e2:
                        if en == "tensor":
                            eng.wait_ge(sems["sDT"], C.get("DTm"))
                        _mark(en, eng, sems[nm], scr=gscr[0:1, 2:3])
                    C.inc(nm[1:], 1)

            return emit_b2

        def make_b3():
            kCb, vb, qCb, qRb = (holder[k] for k in ("kCb", "vb", "qCb", "qRb"))
            er, trib = holder["er"], holder["trib"]
            emr = None
            ub, ubf, lnsb, rssb = (holder[k] for k in ("ub", "ubf", "lnsb", "rssb"))
            obuf = holder["obuf"]
            ps_sc, ps_o, ps_sum = holder["ps_sc"], holder["ps_o"], holder["ps_sum"]

            def pv_tile(en, eng, C, h, s_i, T, so, slot, tt, gstart):
                gt = gstart + tt
                masked = tt >= T - 4
                if en == "tensor":
                    if tt == 0 and slot >= 2:
                        eng.wait_ge(sems["sON"], slot - 1)
                    if masked:
                        eng.wait_ge(sems["sGM"], C.get("GMfor%d" % gt))
                        rhs = emr[:, (tt - (T - 4)) * CH:(tt - (T - 4) + 1) * CH]
                    else:
                        eng.wait_ge(sems["sEX"], gt // 2 + 1)
                        rhs = er[:, (gt % ER) * CH:((gt % ER) + 1) * CH]
                    eng.matmul(ps_o[so][:, :],
                               vb[h % 2][:, tt * 128:(tt + 1) * 128],
                               rhs, start=(tt == 0), stop=(tt == T - 1)
                               ).then_inc(sems["sPVt"], 1)
                C.inc("PVt", 1)

            def pv_tile(en, eng, C, h, so, slot, tt, T, gstart):
                gt = gstart + tt
                if en == "tensor":
                    if tt == 0 and slot >= 2:
                        eng.wait_ge(sems["sON"], slot - 1)
                    rhs = er[:, (gt % ER) * CH:((gt % ER) + 1) * CH]
                    eng.matmul(ps_o[so][:, :],
                               vb[h % 2][:, tt * 128:(tt + 1) * 128],
                               rhs, start=(tt == 0), stop=(tt == T - 1)
                               ).then_inc(sems["sPVt"], 1)
                C.inc("PVt", 1)

            def attn_arm(en, eng, C, chunkset):
                slot = 0
                for h in range(NH):
                    hb = h % 2
                    for s_i, c in enumerate(chunkset):
                        T = 4 * (c + 1)
                        so = slot % 2
                        gstart = C.get("GT")
                        Pb = C.get("PVt")
                        NBT = T // 4
                        for j in range(NBT):
                            gt0 = gstart + 4 * j
                            if en == "tensor":
                                if j == 0 and s_i == 0:
                                    eks = "sEKa" if h % 2 == 0 else "sEKb"
                                    eng.wait_ge(sems[eks], 64 * (h // 2 + 1))
                                if gt0 >= 4:
                                    eng.wait_ge(sems["sEX"], gt0 // 2)
                            for tt in range(4 * j, 4 * j + 4):
                                gt = C.get("GT")
                                masked = tt >= T - 4
                                if en == "tensor":
                                    eng.matmul(ps_sc[:, (gt % 4) * CH:((gt % 4) + 1) * CH],
                                               kCb[hb][:, tt * 128:(tt + 1) * 128],
                                               qCb[hb][:, s_i * CH:(s_i + 1) * CH],
                                               start=True, stop=False)
                                    if masked:
                                        eng.matmul(
                                            ps_sc[:, (gt % 4) * CH:((gt % 4) + 1) * CH],
                                            kRb[:, tt * 128:(tt + 1) * 128],
                                            qRb[hb][:, s_i * CH:(s_i + 1) * CH],
                                            start=False, stop=False)
                                        eng.matmul(
                                            ps_sc[:, (gt % 4) * CH:((gt % 4) + 1) * CH],
                                            ideye[:, :],
                                            trib[:, (tt - (T - 4)) * CH:
                                                 (tt - (T - 4) + 1) * CH],
                                            start=False, stop=True
                                            ).then_inc(sems["sSCt"], 1)
                                    else:
                                        eng.matmul(
                                            ps_sc[:, (gt % 4) * CH:((gt % 4) + 1) * CH],
                                            kRb[:, tt * 128:(tt + 1) * 128],
                                            qRb[hb][:, s_i * CH:(s_i + 1) * CH],
                                            start=False, stop=True
                                            ).then_inc(sems["sSCt"], 1)
                                C.inc("SCt", 1)
                                if gt % 2 == 1:
                                    if en == "scalar":
                                        eng.wait_ge(sems["sSCt"], C.get("SCt"))
                                        if gt >= ER:
                                            eng.wait_ge(sems["sVU"], gt - ER + 2)
                                            eng.wait_ge(sems["sPVt"], gt - ER + 2)
                                        eng.activation(
                                            er[:, ((gt - 1) % ER) * CH:
                                               ((gt - 1) % ER + 2) * CH],
                                            ps_sc[:, ((gt - 1) % 4) * CH:
                                                  ((gt - 1) % 4 + 2) * CH],
                                            AF.Exp, scale=SCALE).then_inc(sems["sEX"], 1)
                                    C.inc("EX", 1)
                                if en == "vector":
                                    eng.wait_ge(sems["sEX"], gt // 2 + 1)
                                    esl = er[:, (gt % ER) * CH:((gt % ER) + 1) * CH]
                                    if tt == 0:
                                        eng.wait_ge(sems["sUB"], C.get("UB"))
                                        eng.tensor_copy(ub[:, :], esl
                                                        ).then_inc(sems["sVU"], 1)
                                    else:
                                        eng.wait_ge(sems["sVU"], C.get("VU"))
                                        eng.tensor_tensor(ub[:, :], ub[:, :], esl,
                                                          ALU.add).then_inc(sems["sVU"], 1)
                                C.inc("VU", 1)
                                C.inc("GT", 1)
                            if j >= 1:
                                for tt in range(4 * (j - 1), 4 * j):
                                    pv_tile(en, eng, C, h, so, slot, tt, T, gstart)
                        if en == "tensor":
                            eng.wait_ge(sems["sEX"], (gstart + T) // 2)
                        for tt in range(T - 4, T):
                            pv_tile(en, eng, C, h, so, slot, tt, T, gstart)
                        if en == "scalar":
                            if slot >= 2:
                                eng.wait_ge(sems["sSM"], slot - 1)
                            eng.wait_ge(sems["sVU"], C.get("VU"))
                            eng.activation(ubf[so][:, :], ub[:, :], AF.Identity
                                           ).then_inc(sems["sUB"], 1)
                        C.inc("UB", 1)
                        if en == "tensor":
                            eng.wait_ge(sems["sUB"], C.get("UB"))
                            if slot >= 2:
                                eng.wait_ge(sems["sSL"], slot - 1)
                            eng.matmul(ps_sum[so][:, :], onesb[:, :], ubf[so][:, :],
                                       start=True, stop=True).then_inc(sems["sSM"], 1)
                        C.inc("SM", 1)
                        if en == "scalar":
                            eng.wait_ge(sems["sSM"], C.get("SM"))
                            if slot >= 2:
                                eng.wait_ge(sems["sON"], slot - 1)
                            eng.activation(lnsb[so][:, :], ps_sum[so][:, :], AF.Ln
                                           ).then_inc(sems["sSL"], 1)
                            eng.wait_ge(sems["sSL"], C.get("SL") + 1)
                            eng.activation(rssb[so][:, :], lnsb[so][:, :], AF.Exp,
                                           scale=-1.0).then_inc(sems["sRD"], 1)
                        C.inc("SL", 1)
                        C.inc("RD", 1)
                        if en == "vector":
                            eng.wait_ge(sems["sRD"], C.get("RD"))
                            eng.wait_ge(sems["sPVt"], C.get("PVt"))
                            eng.tensor_tensor(obuf[h][:, s_i * CH:(s_i + 1) * CH],
                                              ps_o[so][:, :], rssb[so][:, :],
                                              ALU.mult).then_inc(sems["sON"], 1)
                        C.inc("ON", 1)
                        slot += 1

            def emit_b3(en, eng, C):
                if en == "gpsimd":
                    eng.wait_ge(sems["sGt"], C.get("Gt"))
                    eng.wait_ge(sems["sGv"], C.get("Gv"))
                    eng.wait_ge(sems["sGs"], C.get("Gs"))
                    eng.dma_start(trib[:, :], tri_d).then_inc(sems["sWL"], 16)
                    eng.dma_start(wo_bh[...], wo_d[...]).then_inc(sems["sWOC"], 16)
                    eng.memset(qRb[0][RD:128, :], 0.0)
                    eng.memset(qRb[1][RD:128, :], 0.0).then_inc(sems["sMS"], 1)
                C.inc("WL", 16)
                C.inc("WOC", 16)
                C.inc("MS", 1)
                if en == "sync":
                    for g in ("sGt", "sGv", "sGs", "sGg"):
                        eng.wait_ge(sems[g], C.get(g[1:]))
                    eng.wait_ge(sems["sWK"], C.get("WK"))
                    eng.wait_ge(sems["sWV"], C.get("WV"))
                    for h in range(NH):
                        hb = h % 2
                        eks = "sEKa" if hb == 0 else "sEKb"
                        if h >= 2:
                            eng.wait_ge(sems["sON"], 4 * (h - 1))
                        eng.dma_start(kCb[hb][:, :], kT_h[h, :, :]
                                      ).then_inc(sems[eks], 16)
                        eng.dma_start(vb[hb][:, :], v_h[h, :, :]
                                      ).then_inc(sems[eks], 16)
                        eng.dma_start(qCb[hb][:, :], qT_h[h, :, :]
                                      ).then_inc(sems[eks], 16)
                        eng.dma_start(qRb[hb][0:RD, :], qrT_h[h, :, :]
                                      ).then_inc(sems[eks], 16)
                C.inc("EKa", 64 * ((NH + 1) // 2))
                C.inc("EKb", 64 * (NH // 2))
                if en in ("vector", "tensor"):
                    eng.wait_ge(sems["sWL"], C.get("WL"))
                    eng.wait_ge(sems["sMS"], C.get("MS"))
                if en == "vector":
                    eng.wait_ge(sems["sBH"], C.get("BH"))
                    eng.wait_ge(sems["sWQa"], C.get("WQa"))
                    eng.wait_ge(sems["sWQb"], C.get("WQb"))
                if en in ("tensor", "scalar", "vector"):
                    if en == "tensor":
                        eng.wait_ge(sems["sWL"], C.get("WLinit"))
                        eng.wait_ge(sems["sDV"], C.get("DV"))
                    pid = eng.partition_id()
                    idx = eng.compute_val(pid % 2)
                    arm_counts = []
                    base = dict(C.v)
                    for arm in eng.Switch(idx, 2):
                        Ca = Counts()
                        Ca.v = dict(base)
                        attn_arm(en, eng, Ca, CHUNKSETS[arm])
                        arm_counts.append(Ca.v)
                    ok = {k: v for k, v in arm_counts[0].items() if not k.startswith("GMfor")}
                    ok2 = {k: v for k, v in arm_counts[1].items() if not k.startswith("GMfor")}
                    assert ok == ok2, "parity arms must produce equal counts"
                    C.v = arm_counts[0]
                else:
                    attn_arm(en, eng, C, CHUNKSETS[0])

            return emit_b3

        def make_b4():
            wo_b, x1b, ln2st, psF = (holder[k] for k in
                                     ("wo_b", "x1b", "ln2st", "psF"))
            stgF = holder["stg"]

            def emit_b4(en, eng, C):
                return _emit_b4(en, eng, C, holder["ln2T"], holder["psT"],
                                holder["x1r"])

            obuf = holder["obuf"]

            def _emit_b4(en, eng, C, ln2T_b4, psT_b4, x1r_b4):
                if en == "gpsimd":
                    eng.wait_ge(sems["sON"], C.get("ON"))
                    eng.wait_ge(sems["sRD"], C.get("RD"))
                    eng.wait_ge(sems["sWOC"], C.get("WOC"))
                    eng.dma_start(_sb3(wo_b[:, 0:6 * D], 6),
                                  _rows3(wo_bh[0:768, :])).then_inc(sems["sWOL"], 16)
                C.inc("WOL", 16)
                if en == "sync":
                    eng.wait_ge(sems["sON"], C.get("ON"))
                    eng.wait_ge(sems["sRD"], C.get("RD"))
                    eng.wait_ge(sems["sWOC"], C.get("WOC"))
                    eng.dma_start(_sb3(wo_b[:, 6 * D:11 * D], 5),
                                  _rows3(wo_bh[768:1408, :])).then_inc(sems["sWOH"], 16)
                C.inc("WOH", 16)
                if en == "scalar":
                    eng.wait_ge(sems["sON"], C.get("ON"))
                    eng.wait_ge(sems["sRD"], C.get("RD"))
                    eng.wait_ge(sems["sWOC"], C.get("WOC"))
                    eng.dma_start(_sb3(wo_b[:, 11 * D:16 * D], 5),
                                  _rows3(wo_bh[1408:2048, :])).then_inc(sems["sWOH"], 16)
                C.inc("WOH", 16)
                for m in range(NM + 1):
                    fb = m % 2
                    fp = (m - 1) % 2
                    hos = "sHOa" if fb == 0 else "sHOb"
                    xws = "sXWa" if fb == 0 else "sXWb"
                    w2s = "sW2a" if fp == 0 else "sW2b"
                    if m < NM:
                        psp = psF[fb]
                        if en == "sync":
                            if m == 0:
                                eng.wait_ge(sems["sON"], C.get("ON"))
                            if m >= 2:
                                eng.wait_ge(sems["sX1"], m - 1)
                            eng.dma_start(hbuf[fb][:, :],
                                          ho[m * 128:(m + 1) * 128, :]
                                          ).then_inc(sems[hos], 16)
                            if m == 4:
                                eng.wait_ge(sems["sW2a"], 16)
                                eng.dma_start(stgF[0][:, :], ln2_h[0:128, :]
                                              ).then_inc(sems["sLta"], 16)
                                eng.wait_ge(sems["sW2b"], 16)
                                eng.dma_start(stgF[1][:, :], ln2_h[128:256, :]
                                              ).then_inc(sems["sLtc"], 16)
                        C.inc(hos[1:], 16)
                        if m == 4:
                            C.inc("Lta", 16)
                            C.inc("Ltc", 16)
                        if en == "tensor":
                            if m == 0:
                                eng.wait_ge(sems["sWOL"], C.get("WOL"))
                                eng.wait_ge(sems["sWOH"], C.get("WOH"))
                                eng.wait_ge(sems["sON"], C.get("ON"))
                                eng.wait_ge(sems["sRD"], C.get("RD"))
                            if m >= 2:
                                eng.wait_ge(sems["sX1"], m - 1)
                            for k in range(DT):
                                for n in range(4):
                                    mi = eng.matmul(
                                        psp[:, n * CH:(n + 1) * CH],
                                        obuf[k][:, m * 128:(m + 1) * 128],
                                        wo_b[:, k * D + n * CH:k * D + (n + 1) * CH],
                                        start=(k == 0), stop=(k == DT - 1))
                            mi.then_inc(sems["sFM"], 1)
                        C.inc("FM", 1)
                        if en == "vector":
                            eng.wait_ge(sems["sFM"], C.get("FM"))
                            eng.wait_ge(sems[hos], C.get(hos[1:]))
                            if m >= 2:
                                eng.wait_ge(sems[xws], C.get(xws[1:]))
                                eng.wait_ge(sems["sLN1"], C.get("LN1") - 1)
                            eng.tensor_tensor(x1b[fb][:, :], psp[:, :],
                                              hbuf[fb][:, :], ALU.add
                                              ).then_inc(sems["sX1"], 1)
                            eng.wait_ge(sems["sX1"], C.get("X1") + 1)
                            eng.wait_ge(sems["sVA"], C.get("VA"))
                            eng.tensor_scalar(dmyV[:, :], x1b[fb][:, :], 1.0, 0.0,
                                              ALU.mult, ALU.add,
                                              accum_out=statx[fb][:, :]
                                              ).then_inc(sems["sVA"], 1)
                        C.inc("X1", 1)
                        C.inc("VA", 1)
                        if en == "scalar":
                            eng.wait_ge(sems["sX1"], C.get("X1"))
                            eng.dma_start(x1_h[m * 128:(m + 1) * 128, :],
                                          x1b[fb][:, :]).then_inc(sems[xws], 16)
                        C.inc(xws[1:], 16)
                        if en == "scalar":
                            eng.wait_ge(sems["sX1"], C.get("X1"))
                            eng.wait_ge(sems["sSA"], C.get("SA"))
                            eng.activation(dmyS2[:, :], x1b[fb][:, :], AF.Square,
                                           accum_out=statq[fb][:, :]
                                           ).then_inc(sems["sSA"], 1)
                        C.inc("SA", 1)
                    if m >= 1:
                        if en == "vector":
                            eng.wait_ge(sems["sLN1"], C.get("LN1"))
                            eng.wait_ge(sems["sVA"], C.get("VA"))
                            eng.tensor_scalar(dmyV[:, :], lnst[fp][:, :], 1.0,
                                              0.0, ALU.mult, ALU.add,
                                              accum_out=statx[fp][:, :]
                                              ).then_inc(sems["sVA"], 1)
                        C.inc("VA", 1)
                        if en == "scalar":
                            eng.wait_ge(sems["sLN1"], C.get("LN1"))
                            eng.wait_ge(sems["sSA"], C.get("SA"))
                            eng.activation(dmyS2[:, :], lnst[fp][:, :], AF.Square,
                                           accum_out=statq[fp][:, :]
                                           ).then_inc(sems["sSA"], 1)
                        C.inc("SA", 1)
                        stats_chain(en, eng, C, fp)
                        if m == NM:
                            if en == "scalar":
                                for j in range(6):
                                    eng.wait_ge(sems["sTP"], j + 1)
                                    eng.activation(
                                        ln2T_b4[:, j * D:(j + 1) * D],
                                        psT_b4[j % 2][:, :], AF.Identity
                                        ).then_inc(sems["sTC"], 1)
                                eng.wait_ge(sems["sXWb"], C.get("XWb"))
                                eng.wait_ge(sems["sLN1"], C.get("LN1"))
                                eng.dma_start(x1r_b4[0][:, :], x1_h[0:128, :]
                                              ).then_inc(sems["sXRa"], 16)
                            C.inc("TC", 6)
                            C.inc("XRa", 16)
                        if en == "scalar":
                            if m >= 3:
                                eng.wait_ge(sems[w2s], C.get(w2s[1:]))
                        ln_apply(en, eng, C, fp, lnst[fp][:, :],
                                 ln2st[fp][:, :], "sIN2")
                        if en == ("scalar" if m == NM else "sync"):
                            eng.wait_ge(sems["sIN2"], C.get("IN2"))
                            eng.dma_start(ln2_h[(m - 1) * 128:m * 128, :],
                                          ln2st[fp][:, :]).then_inc(sems[w2s], 16)
                        C.inc(w2s[1:], 16)
                    if m < NM:
                        stats_chain(en, eng, C, fb)
                        ln_apply(en, eng, C, fb, x1b[fb][:, :], lnst[fb][:, :],
                                 "sLN1")

            return emit_b4

        def make_b5():
            wq_b, ln2T, bias_b, x1r = (holder[k] for k in
                                       ("wq_b", "ln2T", "bias_b", "x1r"))
            stg = holder["stg"]

            def emit_b5t(en, eng, C):
                psT = holder["psT"]
                if en == "gpsimd":
                    eng.wait_ge(sems["sWQa"], C.get("WQa"))
                    eng.wait_ge(sems["sWQb"], C.get("WQb"))
                    eng.wait_ge(sems["sBH"], 16)
                    eng.wait_ge(sems["sXWa"], C.get("XWa"))
                    eng.wait_ge(sems["sLN1"], C.get("LN1") - 1)
                    eng.dma_start(bias_b[:, :], bias_h[:, :]).then_inc(sems["sBF"], 16)
                    eng.wait_ge(sems["sFM"], C.get("FM"))
                    for qi, qs in enumerate(("sWQL", "sWQH", "sWQ2", "sWQ3")):
                        eng.dma_start(_sb3(wq_b[:, qi * 4 * D:(qi + 1) * 4 * D], 4),
                                      _rows3(wq_h[qi * 512:(qi + 1) * 512, :])
                                      ).then_inc(sems[qs], 16)
                C.inc("WQL", 16)
                C.inc("WQH", 16)
                C.inc("WQ2", 16)
                C.inc("WQ3", 16)
                C.inc("BF", 16)
                if en == "sync":
                    for m_ in range(2, NM + 8):
                        if m_ < NM:
                            par = "sLta" if m_ % 2 == 0 else "sLtc"
                            w2p = "sW2a" if m_ % 2 == 0 else "sW2b"
                            eng.wait_ge(sems[w2p], 16 * (m_ // 2 + 1))
                            eng.wait_ge(sems["sTP"], m_ - 1)
                            eng.dma_start(stg[m_ % 2][:, :],
                                          ln2_h[m_ * 128:(m_ + 1) * 128, :]
                                          ).then_inc(sems[par], 16)
                        if m_ >= 8:
                            mo = m_ - 8
                            fns_ = "sFNa" if mo % 2 == 0 else "sFNb"
                            eng.wait_ge(sems["sOUT"], 2 * mo + 2)
                            eng.dma_start(out_d[mo * 128:(mo + 1) * 128, :],
                                          hbuf[mo % 2][:, :]
                                          ).then_inc(sems[fns_], 16)
                C.inc("Lta", 16 * 7)
                C.inc("Ltc", 16 * 7)
                C.inc("FNa", 16 * 8)
                C.inc("FNb", 16 * 8)
                if en == "scalar":
                    eng.wait_ge(sems["sW2a"], C.get("W2a"))
                    eng.wait_ge(sems["sW2b"], C.get("W2b"))
                    eng.dma_start(x1r[1][:, :], x1_h[128:256, :]
                                  ).then_inc(sems["sXRb"], 16)
                    for j in range(6, NM + 4):
                        if j < NM:
                            eng.wait_ge(sems["sTP"], j + 1)
                            eng.activation(ln2T[:, j * D:(j + 1) * D],
                                           psT[j % 2][:, :], AF.Identity
                                           ).then_inc(sems["sTC"], 1)
                        if j >= 6:
                            mx = j - 4
                            xrs_ = "sXRa" if mx % 2 == 0 else "sXRb"
                            xws_ = "sXWa" if mx % 2 == 0 else "sXWb"
                            eng.wait_ge(sems["sGV"], 2 * (mx - 1))
                            eng.wait_ge(sems[xws_], 16 * (mx // 2 + 1))
                            eng.dma_start(x1r[mx % 2][:, :],
                                          x1_h[mx * 128:(mx + 1) * 128, :]
                                          ).then_inc(sems[xrs_], 16)
                C.inc("TC", 10)
                C.inc("XRa", 16 * 7)
                C.inc("XRb", 16 * 8)
            def emit_b5(en, eng, C):
                psG = holder["psG"]
                psT = holder["psT"]
                if en == "tensor":
                    eng.wait_ge(sems["sX1"], NM - 1)
                    eng.wait_ge(sems["sWL"], C.get("WLinit"))
                LAG = 4
                for step in range(NM + LAG):
                    if step < NM:
                        m_ = step
                        if en == "tensor":
                            s_ = "sLta" if m_ % 2 == 0 else "sLtc"
                            eng.wait_ge(sems[s_], 16 * (m_ // 2 + 1))
                            if m_ >= 2:
                                eng.wait_ge(sems["sTC"], m_ - 1)
                            for k_ in range(DT):
                                ti = eng.transpose(
                                    psT[m_ % 2][:, k_ * 128:(k_ + 1) * 128],
                                    stg[m_ % 2][:, k_ * 128:(k_ + 1) * 128],
                                    ideye[:, :])
                            ti.then_inc(sems["sTP"], 1)
                        C.inc("TP", 1)
                    if step < LAG:
                        continue
                    m = step - LAG
                    gb = m % 2
                    xws = "sXWa" if gb == 0 else "sXWb"
                    xrs = "sXRa" if gb == 0 else "sXRb"
                    fns = "sFNa" if gb == 0 else "sFNb"
                    for h in range(2):
                        if en == "tensor":
                            if h == 0:
                                eng.wait_ge(sems["sTC"], m + 1)
                            if m >= 1:
                                eng.wait_ge(sems["sGV"], 2 * (m - 1) + h + 1)
                            for k in range(DT):
                                if m == 0 and h == 0 and k % 4 == 0:
                                    eng.wait_ge(sems[("sWQL", "sWQH", "sWQ2",
                                                      "sWQ3")[k // 4]], 16)
                                for n in (2 * h, 2 * h + 1):
                                    mi = eng.matmul(
                                        psG[h][:, (n - 2 * h) * CH:
                                               (n - 2 * h + 1) * CH],
                                        ln2T[:, m * D + k * 128:
                                             m * D + (k + 1) * 128],
                                        wq_b[:, k * D + n * CH:
                                             k * D + (n + 1) * CH],
                                        start=(k == 0), stop=(k == DT - 1))
                            mi.then_inc(sems["sGMM"], 1)
                        C.inc("GMM", 1)
                        if en == "vector":
                            eng.wait_ge(sems["sGMM"], C.get("GMM"))
                            eng.wait_ge(sems[xrs], 16 * (m // 2 + 1))
                            if m == 0 and h == 0:
                                eng.wait_ge(sems["sBF"], C.get("BF"))
                            if m >= 2 and h == 0:
                                eng.wait_ge(sems[fns], 16 * (m // 2 - 1 + 1))
                            eng.tensor_tensor(
                                hbuf[gb][:, h * 1024:(h + 1) * 1024],
                                psG[h][:, :],
                                x1r[gb][:, h * 1024:(h + 1) * 1024],
                                ALU.add).then_inc(sems["sGV"], 1)
                            eng.wait_ge(sems["sGV"], C.inc("GV"))
                            eng.tensor_tensor(
                                hbuf[gb][:, h * 1024:(h + 1) * 1024],
                                hbuf[gb][:, h * 1024:(h + 1) * 1024],
                                bias_b[:, h * 1024:(h + 1) * 1024], ALU.add
                                ).then_inc(sems["sOUT"], 1)
                        C.inc("OUT", 1)
                if en == "sync":
                    for nm_ in ("sLSa", "sLSb"):
                        eng.wait_ge(sems[nm_], 16 * C.get(nm_[1:]))
                    for nm_ in ("sHFa", "sHFb", "sT1a", "sT1b", "sWK", "sWV",
                                "sWL", "sWa", "sWb", "sWc", "sWd", "sWOC",
                                "sLta", "sLtc", "sEKa", "sEKb", "sDWa", "sDWb",
                                "sWQa", "sWQb", "sBH", "sWOL", "sWOH",
                                "sHOa", "sHOb", "sW2a", "sW2b", "sXWa", "sXWb",
                                "sWQL", "sWQH", "sWQ2", "sWQ3", "sQW", "sQb", "sQc", "sBF",
                                "sXRa", "sXRb", "sFNa", "sFNb"):
                        eng.wait_ge(sems[nm_], C.get(nm_[1:]))

            return emit_b5t, emit_b5

        with contextlib.ExitStack() as ps1:
            holder["psP"] = ps(ps1, "psP", [128, 4 * CH])
            holder["psQ"] = ps(ps1, "psQ", [128, 4 * CH])
            with contextlib.ExitStack() as s1:
                holder["s1"] = s1
                holder["wA"] = sb(s1, "wA", [128, DT * LAT], BF16)
                holder["wB"] = sb(s1, "wB", [128, LT * D], BF16)
                holder["wC"] = sb(s1, "wC", [128, LT * D], BF16)
                holder["wD"] = sb(s1, "wD", [128, DT * RD], BF16)
                holder["wE"] = sb(s1, "wE", [128, DT * RD], BF16)
                holder["kcosb"] = sb(s1, "kcosb", [RD, S], BF16)
                holder["ksinb"] = sb(s1, "ksinb", [RD, S], BF16)
                holder["qcosb"] = sb(s1, "qcosb", [128, OWN], BF16)
                holder["qsinb"] = sb(s1, "qsinb", [128, OWN], BF16)
                holder["lnT"] = [sb(s1, f"lnT{i}", [128, DT * CH], BF16)
                                 for i in range(2)]
                holder["ckvT"] = [sb(s1, f"ckvT{i}", [128, LT * CH], BF16)
                                  for i in range(2)]
                holder["est"] = sb(s1, "est_t", [128, 16 * CH], BF16)
                holder["vst"] = [sb(s1, f"vst{i}", [128, D], BF16) for i in range(2)]
                holder["krst"] = sb(s1, "krst", [RD, 2 * CH], BF16)
                holder["krt1"] = sb(s1, "krt1", [128, CH], BF16)
                holder["krt2"] = sb(s1, "krt2", [128, CH], BF16)
                emitters.append(make_b1())
            if nblocks <= 1:
                emit_all()
                return nc
            with contextlib.ExitStack() as s1d:
                holder["wtile"] = [sb(s1d, f"wtile{i}", [128, D], F32)
                                   for i in range(4)]
                holder["wzb"] = sb(s1d, "wzb", [128, D], F32)
                holder["wm1"] = sb(s1d, "wm1", [128, D], F32)
                holder["wm2"] = sb(s1d, "wm2", [128, D], F32)
                holder["wqst"] = [sb(s1d, f"wqst{i}", [128, D], BF16)
                                  for i in range(2)]
                holder["asum"] = sb(s1d, "asum", [128, 1], F32)
                holder["atmp"] = sb(s1d, "atmp", [128, 4], F32)
                holder["brow"] = sb(s1d, "brow", [1, D], F32)
                emitters.append(make_b2())
            if nblocks <= 2:
                emit_all()
                return nc
        with contextlib.ExitStack() as s_ob:
            holder["obuf"] = [sb(s_ob, f"obuf{i}", [128, OWN], BF16)
                              for i in range(NH)]
            with contextlib.ExitStack() as s2:
                holder["kCb"] = [sb(s2, f"kCb{i}", [128, S], BF16) for i in range(2)]
                holder["vb"] = [sb(s2, f"vb{i}", [128, S], BF16) for i in range(2)]
                holder["qCb"] = [sb(s2, f"qCb{i}", [128, OWN], BF16)
                                 for i in range(2)]
                holder["qRb"] = [sb(s2, f"qRb{i}", [128, OWN], BF16)
                                 for i in range(2)]
                holder["er"] = sb(s2, "er", [128, ER * CH], BF16)
                holder["trib"] = sb(s2, "trib", [128, 4 * CH], BF16)
                holder["ub"] = sb(s2, "ub", [128, CH], F32)
                holder["ubf"] = [sb(s2, f"ubf{i}", [128, CH], BF16) for i in range(2)]
                holder["lnsb"] = [sb(s2, f"lnsb{i}", [128, CH], F32)
                                  for i in range(2)]
                holder["rssb"] = [sb(s2, f"rssb{i}", [128, CH], F32)
                                  for i in range(2)]
                holder["ps_sc"] = ps(s2, "ps_sc", [128, 4 * CH])
                holder["ps_o"] = [ps(s2, f"ps_o{i}", [128, CH]) for i in range(2)]
                holder["ps_sum"] = [ps(s2, f"ps_sum{i}", [128, CH]) for i in range(2)]
                emitters.append(make_b3())
                if nblocks <= 3:
                    emit_all()
                    return nc
            with contextlib.ExitStack() as s3:
                holder["wo_b"] = sb(s3, "wo_b", [128, DT * D], BF16)
                holder["x1b"] = [sb(s3, f"x1b{i}", [128, D], F32) for i in range(2)]
                holder["ln2st"] = [sb(s3, f"ln2st{i}", [128, D], BF16)
                                   for i in range(2)]
                holder["psF"] = [ps(s3, f"psF{i}", [128, 4 * CH]) for i in range(2)]
                emitters.append(make_b4())
                if nblocks <= 4:
                    emit_all()
                    return nc
        with contextlib.ExitStack() as s4:
            holder["wq_b"] = sb(s4, "wq_b", [128, DT * D], BF16)
            holder["ln2T"] = sb(s4, "ln2T", [128, DT * OWN], BF16)
            holder["bias_b"] = sb(s4, "bias_b", [128, D], F32)
            holder["x1r"] = [sb(s4, f"x1r{i}", [128, D], F32) for i in range(2)]
            eb5t, eb5 = make_b5()
            holder["psT"] = [
                s4.enter_context(nc.psum_tensor(f"psT{i}", [128, D], BF16))
                for i in range(2)]
            holder["psG"] = [ps(s4, f"psG{i}", [128, 2 * CH]) for i in range(2)]
            emitters.append(eb5t)
            emitters.append(eb5)
            emit_all()

    return nc



def _rope_tables():
    theta = 1.0 / (10000.0 ** (np.arange(0, RD, 2, dtype=np.float64) / RD))
    pos = np.arange(S, dtype=np.float64)[:, None] * theta[None, :]
    cos = np.cos(pos).astype(np.float32)
    sin = np.sin(pos).astype(np.float32)
    cosA = np.repeat(cos.T, 2, axis=0)
    sinB = np.repeat(sin.T, 2, axis=0)
    sinB[0::2, :] *= -1.0
    return np.ascontiguousarray(cosA), np.ascontiguousarray(sinB)


def _tri_masks():
    tri = np.zeros((128, 4 * CH), np.float32)
    f = np.arange(CH)[None, :]
    p = np.arange(128)[:, None]
    for i in range(4):
        tri[:, i * CH:(i + 1) * CH] = (f - p >= 128 * i).astype(np.float32)
    return (tri - 1.0) * 1e9


def kernel(L0_attn_h, L0_attn_W_DKV, L0_attn_W_UK, L0_attn_W_KR, L0_attn_W_UV,
           L0_attn_W_DQ, L0_attn_W_UQ, L0_attn_W_QR, L0_attn_W_O,
           L0_ffn_W, L0_ffn_ln_w, L0_ffn_ln_b):
    h = np.ascontiguousarray(np.asarray(L0_attn_h, dtype=np.float32))
    sw = np.arange(RD) ^ 1
    swq = np.concatenate([hh * RD + sw for hh in range(NH)])
    cosA, sinB = _rope_tables()
    tri = _tri_masks()

    common = {
        "wdkv": np.ascontiguousarray(np.asarray(L0_attn_W_DKV, dtype=np.float32)),
        "wuk": np.ascontiguousarray(np.asarray(L0_attn_W_UK, dtype=np.float32)),
        "wkr": np.ascontiguousarray(np.asarray(L0_attn_W_KR, dtype=np.float32)),
        "wkrs": np.ascontiguousarray(np.asarray(L0_attn_W_KR, dtype=np.float32)[:, sw]),
        "wuv": np.ascontiguousarray(np.asarray(L0_attn_W_UV, dtype=np.float32)),
        "wdq": np.ascontiguousarray(np.asarray(L0_attn_W_DQ, dtype=np.float32)),
        "wuq": np.ascontiguousarray(np.asarray(L0_attn_W_UQ, dtype=np.float32)),
        "wqr": np.ascontiguousarray(np.asarray(L0_attn_W_QR, dtype=np.float32)),
        "wqrs": np.ascontiguousarray(np.asarray(L0_attn_W_QR, dtype=np.float32)[:, swq]),
        "wo": np.ascontiguousarray(np.asarray(L0_attn_W_O, dtype=np.float32)),
        "wff": np.ascontiguousarray(np.asarray(L0_ffn_W, dtype=np.float32)),
        "lnwc": np.ascontiguousarray(np.asarray(L0_ffn_ln_w, dtype=np.float32).reshape(DT, 128).T),
        "lnbc": np.ascontiguousarray(np.asarray(L0_ffn_ln_b, dtype=np.float32).reshape(DT, 128).T),
        "kcos": cosA, "ksin": sinB,
        "tri": tri, "onesf": np.ones((128, 1), np.float32),
        "eye": np.eye(128, dtype=np.float32),
    }
    in_maps = []
    own_rows = []
    for core in range(8):
        b, p = core // 2, core % 2
        rows = np.concatenate([np.arange(c * CH, (c + 1) * CH)
                               for c in CHUNKSETS[p]])
        own_rows.append(rows)
        m = dict(common)
        m["hf"] = np.ascontiguousarray(h[b])
        m["ho"] = np.ascontiguousarray(h[b][rows])
        qc = cosA[:, rows]
        qs = sinB[:, rows]
        m["qcos"] = np.ascontiguousarray(np.vstack([qc, qc]))
        m["qsin"] = np.ascontiguousarray(np.vstack([qs, qs]))
        in_maps.append(m)

    nc = build_nc()
    res = run_bass_kernel_spmd(nc, in_maps, core_ids=list(range(8)))
    outp = np.empty((4, S, D), np.float32)
    for core in range(8):
        outp[core // 2][own_rows[core]] = res.results[core]["out"]
    return outp
